# version 01
# baseline (speedup 1.0000x reference)
import contextlib
import numpy as np
import ml_dtypes
import concourse.bass as bass
import concourse.mybir as mybir
from concourse.bass_utils import run_bass_kernel_spmd

F32 = mybir.dt.float32
BF16 = mybir.dt.bfloat16
AF = mybir.ActivationFunctionType
ALU = mybir.AluOpType
AX = mybir.AxisListType.X

C23 = 12582912.0
C52 = 6755399441055744.0

NT, NO, D, HEADS, HD, I8K, TT = 12, 8, 2048, 16, 128, 8192, 128
VTW = 16 * 129
BF = ml_dtypes.bfloat16


class Sem:
    def __init__(self, es, nc, name):
        self.h = es.enter_context(nc.semaphore(name))
        self.v = 0
        self.name = name


class Slab:
    def __init__(self, arena, off, n, tag):
        self.arena, self.off, self.n, self.tag = arena, off, n, tag

    def ap(self, c0=0, n=None, p0=0, pn=128):
        n = self.n - c0 if n is None else n
        assert 0 <= c0 and c0 + n <= self.n, (self.tag, c0, n, self.n)
        return self.arena.t[p0:p0 + pn, self.off + c0: self.off + c0 + n]

    def free(self):
        self.arena.free(self)


class Arena:
    def __init__(self, es, nc, name, cols, dtype):
        self.t = es.enter_context(nc.sbuf_tensor(name, [128, cols], dtype))
        self.free_list = [(0, cols)]
        self.name = name

    def alloc(self, n, tag="", high=False):
        n = (n + 3) & ~3
        idxs = range(len(self.free_list))
        for i in (reversed(idxs) if high else idxs):
            off, sz = self.free_list[i]
            if sz >= n:
                if high:
                    self.free_list[i] = (off, sz - n)
                    off = off + sz - n
                else:
                    self.free_list[i] = (off + n, sz - n)
                if self.free_list[i][1] == 0:
                    del self.free_list[i]
                return Slab(self, off, n, tag)
        raise MemoryError(f"{self.name}: alloc {n} ({tag}); free={self.free_list}")

    def free(self, slab):
        self.free_list.append((slab.off, slab.n))
        self.free_list.sort()
        merged = []
        for off, sz in self.free_list:
            if merged and merged[-1][0] + merged[-1][1] == off:
                merged[-1] = [merged[-1][0], merged[-1][1] + sz]
            else:
                merged.append([off, sz])
        self.free_list = [tuple(t) for t in merged]


class MK:
    def __init__(self, debug=()):
        self.nc = bass.Bass("TRN2", target_bir_lowering=False, detect_race_conditions=False)
        self.es = contextlib.ExitStack()
        self.debug = set(debug)
        self.q = {k: [] for k in ("sync", "tensor", "vector", "scalar", "gpsimd")}
        self.last_wait = {}
        self.dbg_outs = []

    def E(self, eng, fn, waits=(), inc=None, inc_n=1):
        wl = []
        for s in waits:
            key = (eng, s.name)
            if s.v > 0 and self.last_wait.get(key, -1) < s.v:
                wl.append((s.h, s.v))
                self.last_wait[key] = s.v
        if inc is not None:
            inc.v += inc_n

        def run(e, fn=fn, wl=wl, inc=inc, inc_n=inc_n):
            for h, v in wl:
                e.wait_ge(h, v)
            inst = fn(e)
            if inc is not None:
                inst.then_inc(inc.h, inc_n)
        self.q[eng].append(run)

    def W(self, eng, *sems):
        self.E(eng, lambda e: None, waits=sems)

    def dbg(self, name, ap_fn, shape, dtype, waits=()):
        if name not in self.debug:
            return
        t = self.nc.dram_tensor("dbg_" + name, list(shape), dtype,
                                kind="ExternalOutput").ap()
        self.dbg_outs.append("dbg_" + name)
        self.E("sync", lambda e: e.dma_start(out=t, in_=ap_fn()),
               waits=waits, inc=self.sST, inc_n=16)


def build(debug=()):
    m = MK(debug)
    nc, es = m.nc, m.es
    dt = nc.dram_tensor

    x = dt("x", [NT * TT, D], F32, kind="ExternalInput").ap()
    wqT = dt("wqT", [D, D], F32, kind="ExternalInput").ap()
    wkT = dt("wkT", [D, D], F32, kind="ExternalInput").ap()
    wvT = dt("wvT", [D, D], F32, kind="ExternalInput").ap()
    woT = dt("woT", [D, D], F32, kind="ExternalInput").ap()
    wgT = dt("wgT", [D, I8K], F32, kind="ExternalInput").ap()
    wuT = dt("wuT", [D, I8K], F32, kind="ExternalInput").ap()
    wdTp = dt("wdTp", [I8K, D], F32, kind="ExternalInput").ap()
    tabs = {}
    for nm in ("cq", "sq", "ck", "sk"):
        for p in ("hi", "lo"):
            tabs[nm + p] = dt(f"{nm}_{p}", [NT * TT, 64], F32,
                              kind="ExternalInput").ap()
    maskT = dt("maskT", [NO * 5 * TT, TT], BF16, kind="ExternalInput").ap()
    h128 = dt("h128", [128, 128], BF16, kind="ExternalInput").ap()
    h64d = dt("h64", [128, 64], BF16, kind="ExternalInput").ap()
    idenf = dt("idenf", [128, 128], F32, kind="ExternalInput").ap()
    chi = dt("chi", [128, 8], F32, kind="ExternalInput").ap()
    clo = dt("clo", [128, 8], F32, kind="ExternalInput").ap()
    out = dt("out", [NO * TT, D], F32, kind="ExternalOutput").ap()
    xq_sp = dt("xq_spill", [NO * TT, D], BF16, kind="Internal").ap()
    v_sp = dt("v_spill", [128, NT * VTW], BF16, kind="Internal").ap()
    q_sp = dt("q_spill", [128, HEADS * NO * TT], BF16, kind="Internal").ap()

    AB = Arena(es, nc, "AB", 86400, BF16)
    AFr = Arena(es, nc, "AF", 9950, F32)
    PSA = es.enter_context(nc.psum_tensor("psA", [128, 1024], F32))
    PSB = es.enter_context(nc.psum_tensor("psB", [128, 1024], F32))
    PSC = es.enter_context(nc.psum_tensor("psC", [128, 1024], F32))
    PSD = es.enter_context(nc.psum_tensor("psD", [128, 1024], F32))
    PS4 = [PSA, PSB, PSC, PSD]

    for nm in ("sLD", "sLDG", "sV", "sS", "sP", "sG", "sST", "sSTG"):
        setattr(m, nm, Sem(es, nc, nm))
    E, W = m.E, m.W
    sLD, sLDG, sV, sS, sP, sST = m.sLD, m.sLDG, m.sV, m.sS, m.sP, m.sST
    sSTG = m.sSTG

    c_iden = AFr.alloc(128, "iden")
    c_sm = AFr.alloc(32, "small")
    c_hi = AFr.alloc(8, "chi")
    c_lo = AFr.alloc(8, "clo")
    c_ones = AFr.alloc(128, "onesf")
    sb_h128 = AB.alloc(128, "h128")
    sb_h64 = AB.alloc(64, "h64")
    sb_tab = {nm: AB.alloc(NT * 64, "tb" + nm) for nm in ("cq", "sq", "ck", "sk")}
    sb_thl = AFr.alloc(2 * NT * 64, "tabhl")

    E("sync", lambda e: e.dma_start(out=c_iden.ap(), in_=idenf), inc=sLD, inc_n=16)
    E("sync", lambda e: e.dma_start(out=c_hi.ap(), in_=chi), inc=sLD, inc_n=16)
    E("sync", lambda e: e.dma_start(out=c_lo.ap(), in_=clo), inc=sLD, inc_n=16)
    E("sync", lambda e: e.dma_start(out=sb_h128.ap(), in_=h128), inc=sLD, inc_n=16)
    E("sync", lambda e: e.dma_start(out=sb_h64.ap(), in_=h64d), inc=sLD, inc_n=16)
    E("vector", lambda e: e.memset(c_ones.ap(), 1.0), inc=sV)
    E("vector", lambda e: e.memset(c_sm.ap(4, 1), 1.0), waits=(sLD,), inc=sV)
    E("vector", lambda e: e.tensor_scalar(
        out=c_sm.ap(0, 1), in0=c_sm.ap(4, 1), scalar1=C52, scalar2=-C52,
        op0=ALU.add, op1=ALU.add), waits=(sV,), inc=sV)
    E("vector", lambda e: e.tensor_scalar(
        out=c_sm.ap(0, 1), in0=c_sm.ap(0, 1), scalar1=0.0, scalar2=1.0,
        op0=ALU.is_equal, op1=ALU.mult), waits=(sV,), inc=sV)
    E("vector", lambda e: e.tensor_tensor(
        out=c_sm.ap(1, 3), in0=c_hi.ap(1, 3), in1=c_lo.ap(1, 3), op=ALU.add),
        inc=sV)
    c_hi.free(); c_lo.free()
    for nm in ("cq", "sq", "ck", "sk"):
        E("sync", lambda e, nm=nm: e.dma_start(
            out=sb_thl.ap(0, NT * 64).rearrange("p (t f) -> p t f", f=64),
            in_=tabs[nm + "hi"].rearrange("(t p) f -> p t f", p=128)),
            waits=(sV,), inc=sLD, inc_n=16)
        E("sync", lambda e, nm=nm: e.dma_start(
            out=sb_thl.ap(NT * 64, NT * 64).rearrange("p (t f) -> p t f", f=64),
            in_=tabs[nm + "lo"].rearrange("(t p) f -> p t f", p=128)),
            inc=sLD, inc_n=16)
        E("vector", lambda e, nm=nm: e.tensor_tensor(
            out=sb_tab[nm].ap(), in0=sb_thl.ap(0, NT * 64),
            in1=sb_thl.ap(NT * 64, NT * 64), op=ALU.add),
            waits=(sLD,), inc=sV)
        W("sync", sV)
    sb_thl.free()

    sb_xnTh = AB.alloc(4 * 16 * TT, "xnT_halo")
    sb_xnTo = AB.alloc(NO * 16 * TT, "xnT_own")

    def xnT_ap(t, f):
        if t < 4:
            return sb_xnTh.ap((t * 16 + f) * TT, TT)
        return sb_xnTo.ap(((t - 4) * 16 + f) * TT, TT)

    sb_xt = AFr.alloc(D, "x_t")
    sb_xn = AFr.alloc(D, "xn_t")
    sb_u = AFr.alloc(1024, "u_t")
    sb_aq = AFr.alloc(1024, "a_t")
    sb_bq = AFr.alloc(1024, "b_t")
    sb_ln = AFr.alloc(16, "ln", high=True)
    sb_mxc = AFr.alloc(8, "mxc", high=True)
    sb_xq1 = AB.alloc(D, "xq1")

    def quant_pieces(src_slab_ap_fn, scol, out_ap_fn, total, piece):
        for c0 in range(0, total, piece):
            E("vector", lambda e, c0=c0: e.tensor_scalar(
                out=sb_u.ap(0, piece), in0=src_slab_ap_fn(c0, piece),
                scalar1=scol, scalar2=127.4, op0=ALU.mult, op1=ALU.min),
                waits=(sLDG, sV), inc=sV)
            E("vector", lambda e: e.tensor_scalar(
                out=sb_aq.ap(0, piece), in0=sb_u.ap(0, piece), scalar1=C23,
                scalar2=-C23, op0=ALU.add, op1=ALU.add), inc=sV)
            E("vector", lambda e: e.tensor_scalar(
                out=sb_bq.ap(0, piece), in0=sb_u.ap(0, piece), scalar1=C52,
                scalar2=-C52, op0=ALU.add, op1=ALU.add), inc=sV)
            E("vector", lambda e, c0=c0: e.scalar_tensor_tensor(
                out=out_ap_fn(c0, piece), in0=sb_aq.ap(0, piece),
                scalar=c_sm.ap(0, 1), in1=sb_bq.ap(0, piece),
                op0=ALU.mult, op1=ALU.add), waits=(sV,), inc=sV)

    for t in range(NT):
        E("sync", lambda e, t=t: e.dma_start(
            out=sb_xt.ap(), in_=x[t * TT:(t + 1) * TT, :]),
            waits=(sV, sS), inc=sLD, inc_n=16)
        E("vector", lambda e: e.reduce_sum(
            out=sb_ln.ap(0, 1), in_=sb_xt.ap(), axis=AX), waits=(sLD,), inc=sV)
        E("scalar", lambda e: e.activation(
            out=sb_xn.ap(), in_=sb_xt.ap(), func=AF.Square, bias=0.0, scale=1.0,
            accum_out=sb_ln.ap(3, 1)), waits=(sLD, sV, sP), inc=sS)
        E("vector", lambda e: e.tensor_scalar(
            out=sb_ln.ap(1, 1), in0=sb_ln.ap(0, 1), scalar1=1.0 / D,
            scalar2=-1.0, op0=ALU.mult, op1=ALU.mult), waits=(sV,), inc=sV)
        E("vector", lambda e: e.tensor_tensor(
            out=sb_ln.ap(4, 1), in0=sb_ln.ap(1, 1), in1=sb_ln.ap(1, 1),
            op=ALU.mult), waits=(sS, sV), inc=sV)
        E("vector", lambda e: e.scalar_tensor_tensor(
            out=sb_ln.ap(4, 1), in0=sb_ln.ap(3, 1), scalar=1.0 / D,
            in1=sb_ln.ap(4, 1), op0=ALU.mult, op1=ALU.subtract),
            waits=(sV,), inc=sV)
        E("vector", lambda e: e.tensor_tensor(
            out=sb_ln.ap(4, 1), in0=sb_ln.ap(4, 1), in1=c_sm.ap(1, 1),
            op=ALU.add), waits=(sV,), inc=sV)
        E("scalar", lambda e: e.activation(
            out=sb_ln.ap(5, 1), in_=sb_ln.ap(4, 1), func=AF.Sqrt, bias=0.0,
            scale=1.0), waits=(sV,), inc=sS)
        E("vector", lambda e: e.reciprocal(
            out=sb_ln.ap(6, 1), in_=sb_ln.ap(5, 1)), waits=(sS,), inc=sV)
        E("vector", lambda e: e.tensor_scalar(
            out=sb_xn.ap(), in0=sb_xt.ap(), scalar1=sb_ln.ap(1, 1),
            scalar2=sb_ln.ap(6, 1), op0=ALU.add, op1=ALU.mult),
            waits=(sP, sV), inc=sV)
        if t >= 4:
            o = t - 4
            E("vector", lambda e: e.reduce_max(
                out=sb_ln.ap(8, 1), in_=sb_xn.ap(), axis=AX,
                apply_absolute_value=True), inc=sV)
            E("vector", lambda e, o=o: e.tensor_tensor(
                out=sb_mxc.ap(o, 1), in0=sb_ln.ap(8, 1), in1=c_sm.ap(2, 1),
                op=ALU.max), waits=(sV,), inc=sV)
            E("vector", lambda e, o=o: e.reciprocal(
                out=sb_ln.ap(10, 1), in_=sb_mxc.ap(o, 1)), waits=(sV,), inc=sV)
            E("vector", lambda e: e.tensor_scalar(
                out=sb_ln.ap(11, 1), in0=sb_ln.ap(10, 1), scalar1=128.0,
                scalar2=1.0, op0=ALU.mult, op1=ALU.mult), waits=(sV,), inc=sV)
            quant_pieces(lambda c0, n: sb_xn.ap(c0, n), sb_ln.ap(11, 1),
                         lambda c0, n: sb_xq1.ap(c0, n), D, 1024)
            E("gpsimd", lambda e, o=o: e.dma_start(
                out=xq_sp[o * TT:(o + 1) * TT, :], in_=sb_xq1.ap()),
                waits=(sV,), inc=sLDG, inc_n=16)
        def T1(f, t=t):
            half = (f % 2) * 512
            E("tensor", lambda e, f=f, half=half: e.transpose(
                PSA[:, half:half + 128], sb_xn.ap(f * 128, 128), c_iden.ap()),
                waits=(sV,), inc=sP)

        def E1(f, t=t):
            half = (f % 2) * 512
            E("vector", lambda e, f=f, t=t, half=half: e.tensor_copy(
                out=xnT_ap(t, f), in_=PSA[:, half:half + 128]),
                waits=(sP,), inc=sV)
        T1(0)
        for f in range(1, 16):
            T1(f)
            E1(f - 1)
        E1(15)

    m.dbg("mxc", lambda: sb_mxc.ap(), [128, 8], F32, waits=(sV,))
    for s in (sb_xt, sb_u, sb_aq, sb_bq, sb_xq1, sb_ln):
        s.free()
    sb_xn.free()

    sb_Kf = AB.alloc(HEADS * NT * TT, "Kfm")
    sb_wsl = AB.alloc(16 * 256, "wslice")
    sb_rp = AFr.alloc(256, "rope_scr")
    sb_t1 = AFr.alloc(128, "rope_t1")
    sb_t2 = AFr.alloc(128, "rope_t2")
    sb_tab4 = {nm: AB.alloc(2 * 64, "t4" + nm) for nm in ("cq", "sq", "ck", "sk")}
    sb_ve = AB.alloc(2 * 129, "v_evac")
    sb_qe = AB.alloc(TT, "q_evac")
    E("vector", lambda e: e.memset(
        sb_ve.ap(0, 258).rearrange("p (h c) -> p h c", c=129)[:, :, 128:129], 1.0),
        inc=sV)

    def proj(kind, wT, tiles):
        cn, sn = ("cq", "sq") if kind == 'q' else ("ck", "sk")
        for oq in range(8):
            E("gpsimd", lambda e, oq=oq: e.dma_start(
                out=sb_wsl.ap(),
                in_=wT[:, oq * 256:(oq + 1) * 256].rearrange(
                    "(c p) o -> p c o", p=128),
                out_override=None) if False else e.dma_start(
                out=sb_wsl.ap().rearrange("p (c o) -> p c o", o=256),
                in_=wT[:, oq * 256:(oq + 1) * 256].rearrange(
                    "(c p) o -> p c o", p=128)),
                waits=(sV, sP), inc=sLDG, inc_n=16)

            def mm(ti):
                t = tiles[ti]
                bank = (ti % 2) * 512
                for f in range(16):
                    E("tensor", lambda e, f=f, t=t, bank=bank: e.matmul(
                        PSB[:, bank:bank + 256], xnT_ap(t, f),
                        sb_wsl.ap(f * 256, 256),
                        start=(f == 0), stop=(f == 15)),
                        waits=((sLDG, sV) if f == 0 else ()),
                        inc=(sP if f == 15 else None))

            def consume(ti):
                t = tiles[ti]
                bank = (ti % 2) * 512
                if kind == 'v':
                    E("vector", lambda e, t=t, oq=oq, bank=bank: e.tensor_copy(
                        out=sb_ve.ap(0, 258).rearrange(
                            "p (h c) -> p h c", c=129)[:, :, 0:128],
                        in_=PSB[:, bank:bank + 256].rearrange(
                            "p (h c) -> p h c", c=128)),
                        waits=(sP, sLDG), inc=sV)
                    E("gpsimd", lambda e, t=t, oq=oq: e.dma_start(
                        out=v_sp[:, t * VTW + oq * 2 * 129:
                                 t * VTW + oq * 2 * 129 + 129 * 2],
                        in_=sb_ve.ap(0, 258)), waits=(sV,), inc=sLDG, inc_n=16)
                    return
                base = PSB[:, bank:bank + 256].rearrange(
                    "p (h d) -> p h d", d=128)
                ct = sb_tab4[cn].ap()
                st = sb_tab4[sn].ap()
                ctr = ct.rearrange("p (h f) -> p h f", f=64)
                str_ = st.rearrange("p (h f) -> p h f", f=64)
                rp2 = sb_rp.ap().rearrange("p (h d) -> p h d", d=128)
                t1r = sb_t1.ap().rearrange("p (h f) -> p h f", f=64)
                t2r = sb_t2.ap().rearrange("p (h f) -> p h f", f=64)
                E("vector", lambda e: e.tensor_tensor(
                    out=t1r[:, :, :], in0=base[:, :, 0:64], in1=ctr[:, :, :],
                    op=ALU.mult), waits=(sP, sV), inc=sV)
                E("vector", lambda e: e.tensor_tensor(
                    out=t2r[:, :, :], in0=base[:, :, 64:128], in1=str_[:, :, :],
                    op=ALU.mult), inc=sV)
                E("vector", lambda e: e.tensor_tensor(
                    out=rp2[:, :, 0:64], in0=t1r[:, :, :], in1=t2r[:, :, :],
                    op=ALU.subtract), waits=(sV,), inc=sV)
                E("vector", lambda e: e.tensor_tensor(
                    out=t1r[:, :, :], in0=base[:, :, 0:64], in1=str_[:, :, :],
                    op=ALU.mult), inc=sV)
                E("vector", lambda e: e.tensor_tensor(
                    out=t2r[:, :, :], in0=base[:, :, 64:128], in1=ctr[:, :, :],
                    op=ALU.mult), inc=sV)
                E("vector", lambda e: e.tensor_tensor(
                    out=rp2[:, :, 64:128], in0=t1r[:, :, :], in1=t2r[:, :, :],
                    op=ALU.add), waits=(sV,), inc=sV)
                for j in range(2):
                    hdi = oq * 2 + j
                    pb = 512 + (j % 2) * 128
                    E("tensor", lambda e, j=j, pb=pb: e.transpose(
                        PSA[:, pb:pb + 128], sb_rp.ap(j * 128, 128),
                        c_iden.ap()), waits=(sV,), inc=sP)
                    if kind == 'k':
                        E("vector", lambda e, hdi=hdi, t=t, pb=pb: e.tensor_copy(
                            out=sb_Kf.ap(hdi * NT * TT + t * TT, TT),
                            in_=PSA[:, pb:pb + 128]), waits=(sP,), inc=sV)
                    else:
                        E("vector", lambda e, pb=pb: e.tensor_copy(
                            out=sb_qe.ap(), in_=PSA[:, pb:pb + 128]),
                            waits=(sP, sLDG), inc=sV)
                        E("gpsimd", lambda e, hdi=hdi, t=t: e.dma_start(
                            out=q_sp[:, hdi * NO * TT + (t - 4) * TT:
                                     hdi * NO * TT + (t - 4) * TT + TT],
                            in_=sb_qe.ap()), waits=(sV,), inc=sLDG, inc_n=16)

            if kind != 'v':
                pass
            mm(0)
            for ti in range(1, len(tiles)):
                if kind != 'v':
                    tt_ = tiles[ti - 1]
                    for nm in (cn, sn):
                        for j in range(2):
                            E("vector", lambda e, nm=nm, j=j, tt_=tt_:
                              e.tensor_copy(
                                out=sb_tab4[nm].ap(j * 64, 64),
                                in_=sb_tab[nm].ap(tt_ * 64, 64)), inc=sV)
                mm(ti)
                consume(ti - 1)
            if kind != 'v':
                tt_ = tiles[-1]
                for nm in (cn, sn):
                    for j in range(2):
                        E("vector", lambda e, nm=nm, j=j, tt_=tt_: e.tensor_copy(
                            out=sb_tab4[nm].ap(j * 64, 64),
                            in_=sb_tab[nm].ap(tt_ * 64, 64)), inc=sV)
            consume(len(tiles) - 1)

    proj('v', wvT, list(range(NT)))
    proj('k', wkT, list(range(NT)))
    sb_xnTh.free()
    proj('q', wqT, list(range(4, NT)))

    m.dbg("Kfm", lambda: sb_Kf.ap(), [128, HEADS * NT * TT], BF16, waits=(sV,))

    sb_wsl.free()
    sb_xnTo.free()
    for s in (sb_t1, sb_t2, sb_rp, sb_ve, sb_qe):
        s.free()
    for s in sb_tab.values():
        s.free()
    for s in sb_tab4.values():
        s.free()

    sb_V = AB.alloc(NT * VTW, "Vtm")
    sb_att = AB.alloc(HEADS * NO * TT, "attnfm")
    sb_mask = AB.alloc(NO * 5 * TT, "masks")
    sb_qt = [AB.alloc(TT, "qtile0"), AB.alloc(TT, "qtile1")]
    sb_P = [AFr.alloc(640, "P0"), AFr.alloc(640, "P1")]
    sb_Pt = [AB.alloc(5 * TT, "Pt0"), AB.alloc(5 * TT, "Pt1")]
    sb_nm2 = AFr.alloc(8, "negmax")
    sb_an = [AFr.alloc(TT, "an0"), AFr.alloc(TT, "an1")]
    sb_wab = AFr.alloc(2048, "wabs")
    sb_wjk = AFr.alloc(2048, "wjunk")
    sb_asum = AFr.alloc(64, "asum", high=True)

    E("sync", lambda e: e.dma_start(out=sb_V.ap(), in_=v_sp[:, :]),
      waits=(sLDG,), inc=sLD, inc_n=16)
    E("sync", lambda e: e.dma_start(
        out=sb_mask.ap().rearrange("p (a k) -> p a k", k=TT),
        in_=maskT.rearrange("(a p) k -> p a k", p=128)),
        inc=sLD, inc_n=16)
    E("vector", lambda e: e.memset(
        sb_V.ap().rearrange("p (t h c) -> p t h c", h=16, c=129)[:, :, :, 128:129],
        1.0), waits=(sLD,), inc=sV)

    abs_state = {"mat": 0, "chunk": 0}
    MATS = [(wgT, 5, 7), (wuT, 8, 10), (wdTp, 11, 13)]

    def abs_step():
        mi, ch = abs_state["mat"], abs_state["chunk"]
        if mi >= 3:
            return
        wT, mhc, mc = MATS[mi]
        if ch < 64:
            if mi < 2:
                src = wT[ch * 32:(ch + 1) * 32, :].rearrange(
                    "r (q i) -> (r q) i", q=4)
            else:
                src = wT[ch * 128:(ch + 1) * 128, :]
            E("gpsimd", lambda e, src=src: e.dma_start(
                out=sb_wab.ap(), in_=src), waits=(sS,), inc=sLDG, inc_n=16)
            E("scalar", lambda e, ch=ch: e.activation(
                out=sb_wjk.ap(), in_=sb_wab.ap(), func=AF.Abs, bias=0.0,
                scale=1.0, accum_out=sb_asum.ap(ch, 1)),
                waits=(sLDG,), inc=sS)
            abs_state["chunk"] += 1
        else:
            E("vector", lambda e: e.reduce_sum(
                out=sb_nm2.ap(4, 1), in_=sb_asum.ap(), axis=AX),
                waits=(sS,), inc=sV)
            E("tensor", lambda e: e.matmul(
                PSD[:, 384:385], c_ones.ap(), sb_nm2.ap(4, 1),
                start=True, stop=True), waits=(sV,), inc=sP)
            E("vector", lambda e, mc=mc: e.tensor_scalar(
                out=sb_nm2.ap(5, 1), in0=PSD[:, 384:385], scalar1=2.0 ** -24,
                scalar2=1.0, op0=ALU.mult, op1=ALU.mult), waits=(sP,), inc=sV)
            E("vector", lambda e, mc=mc: e.tensor_tensor(
                out=c_sm.ap(mc, 1), in0=sb_nm2.ap(5, 1), in1=c_sm.ap(2, 1),
                op=ALU.max), waits=(sV,), inc=sV)
            E("vector", lambda e, mhc=mhc, mc=mc: e.tensor_scalar(
                out=c_sm.ap(mhc, 1), in0=c_sm.ap(mc, 1), scalar1=0.5,
                scalar2=1.0, op0=ALU.mult, op1=ALU.mult), waits=(sV,), inc=sV)
            E("vector", lambda e, mhc=mhc, mc=mc: e.tensor_scalar(
                out=c_sm.ap(mhc + 1, 1), in0=c_sm.ap(mc, 1), scalar1=-0.5,
                scalar2=1.0, op0=ALU.mult, op1=ALU.mult), waits=(sV,), inc=sV)
            abs_state["mat"] += 1
            abs_state["chunk"] = 0

    SC = [PSA, PSB]
    PVo = [0, 512]
    ATo = [256, 768]
    PTo = [(0, 128), (512, 640)]

    def attn_A(idx):
        hd, qt = divmod(idx, NO)
        p = idx % 2
        E("sync", lambda e, hd=hd, qt=qt, p=p: e.dma_start(
            out=sb_qt[p].ap(),
            in_=q_sp[:, hd * NO * TT + qt * TT: hd * NO * TT + qt * TT + TT]),
            waits=(sP,), inc=sLD, inc_n=16)
        E("tensor", lambda e, hd=hd, qt=qt, p=p: e.matmul(
            SC[p][:, 0:512], sb_qt[p].ap(),
            sb_Kf.ap(hd * NT * TT + qt * TT, 512),
            start=True, stop=True), waits=(sLD, sV, sS), inc=sP)
        E("tensor", lambda e, hd=hd, qt=qt, p=p: e.matmul(
            SC[p][:, 512:640], sb_qt[p].ap(),
            sb_Kf.ap(hd * NT * TT + qt * TT + 512, 128),
            start=True, stop=True), inc=sP)
        E("vector", lambda e, p=p: e.reduce_max(
            out=sb_nm2.ap(p, 1), in_=SC[p][:, 0:640], axis=AX, negate=True),
            waits=(sP,), inc=sV)
        E("scalar", lambda e, p=p: e.activation(
            out=sb_P[p].ap(), in_=SC[p][:, 0:640], func=AF.Exp,
            bias=sb_nm2.ap(p, 1), scale=1.0), waits=(sV,), inc=sS)

    def attn_B(idx):
        hd, qt = divmod(idx, NO)
        p = idx % 2

        def TP(s5):
            pb = PTo[p][s5 % 2]
            E("tensor", lambda e, s5=s5, p=p, pb=pb: e.transpose(
                PSD[:, pb:pb + 128], sb_P[p].ap(s5 * 128, 128), c_iden.ap()),
                waits=(sS, sV), inc=sP)

        def EV(s5):
            pb = PTo[p][s5 % 2]
            E("vector", lambda e, s5=s5, qt=qt, p=p, pb=pb: e.tensor_tensor(
                out=sb_Pt[p].ap(s5 * TT, TT), in0=PSD[:, pb:pb + 128],
                in1=sb_mask.ap((qt * 5 + s5) * TT, TT), op=ALU.mult),
                waits=(sP,), inc=sV)
        TP(0)
        for s5 in range(1, 5):
            TP(s5)
            EV(s5 - 1)
        EV(4)
        for s5 in range(5):
            E("tensor", lambda e, s5=s5, qt=qt, hd=hd, p=p: e.matmul(
                PSC[:, PVo[p]:PVo[p] + 129], sb_Pt[p].ap(s5 * TT, TT),
                sb_V.ap((qt + s5) * VTW + hd * 129, 129),
                start=(s5 == 0), stop=(s5 == 4)),
                waits=((sV,) if s5 == 0 else ()),
                inc=(sP if s5 == 4 else None))
        E("vector", lambda e, p=p: e.reciprocal(
            out=sb_nm2.ap(2 + p, 1), in_=PSC[:, PVo[p] + 128:PVo[p] + 129]),
            waits=(sP,), inc=sV)
        E("vector", lambda e, p=p: e.tensor_scalar(
            out=sb_an[p].ap(), in0=PSC[:, PVo[p]:PVo[p] + 128],
            scalar1=sb_nm2.ap(2 + p, 1), scalar2=1.0, op0=ALU.mult,
            op1=ALU.mult), waits=(sV,), inc=sV)
        E("tensor", lambda e, p=p: e.transpose(
            PSC[:, ATo[p]:ATo[p] + 128], sb_an[p].ap(), c_iden.ap()),
            waits=(sV,), inc=sP)
        E("vector", lambda e, hd=hd, qt=qt, p=p: e.tensor_copy(
            out=sb_att.ap(hd * NO * TT + qt * TT, TT),
            in_=PSC[:, ATo[p]:ATo[p] + 128]), waits=(sP,), inc=sV)

    NTOT = HEADS * NO
    attn_A(0)
    for i in range(1, NTOT):
        attn_A(i)
        attn_B(i - 1)
        if i % 8 == 0:
            for _ in range(12):
                abs_step()
    attn_B(NTOT - 1)
    while abs_state["mat"] < 3:
        abs_step()

    m.dbg("attnfm", lambda: sb_att.ap(), [128, HEADS * NO * TT], BF16,
          waits=(sV,))
    m.dbg("msum", lambda: c_sm.ap(0, 14), [128, 14], F32, waits=(sV,))

    for s in sb_P + sb_Pt + sb_an + sb_qt:
        s.free()
    sb_nm2.free(); sb_V.free(); sb_Kf.free(); sb_mask.free()
    sb_wab.free(); sb_wjk.free(); sb_asum.free()

    sb_wsl2 = AB.alloc(16 * 512, "wo_slice")
    sb_xo = [AFr.alloc(512, "x_own0"), AFr.alloc(512, "x_own1")]
    sb_ot = AFr.alloc(512, "out_t")
    for oh in range(4):
        E("gpsimd", lambda e, oh=oh: e.dma_start(
            out=sb_wsl2.ap().rearrange("p (c o) -> p c o", o=512),
            in_=woT[:, oh * 512:(oh + 1) * 512].rearrange(
                "(c p) o -> p c o", p=128)), waits=(sV,), inc=sLDG, inc_n=16)

        def wo_mm(qt, oh=oh):
            bank = (qt % 2) * 512
            for hh in range(16):
                E("tensor", lambda e, hh=hh, qt=qt, bank=bank: e.matmul(
                    PSA[:, bank:bank + 512],
                    sb_att.ap(hh * NO * TT + qt * TT, TT),
                    sb_wsl2.ap(hh * 512, 512),
                    start=(hh == 0), stop=(hh == 15)),
                    waits=((sLDG, sV) if hh == 0 else ()),
                    inc=(sP if hh == 15 else None))
            E("sync", lambda e, qt=qt, oh=oh: e.dma_start(
                out=sb_xo[qt % 2].ap(),
                in_=x[(qt + 4) * TT:(qt + 5) * TT, oh * 512:(oh + 1) * 512]),
                waits=(sV,), inc=sLD, inc_n=16)

        def wo_ev(qt, oh=oh):
            bank = (qt % 2) * 512
            E("vector", lambda e, qt=qt, bank=bank: e.tensor_tensor(
                out=sb_ot.ap(), in0=PSA[:, bank:bank + 512],
                in1=sb_xo[qt % 2].ap(), op=ALU.add),
                waits=(sP, sLD, sST), inc=sV)
            E("sync", lambda e, qt=qt, oh=oh: e.dma_start(
                out=out[qt * TT:(qt + 1) * TT, oh * 512:(oh + 1) * 512],
                in_=sb_ot.ap()), waits=(sV,), inc=sST, inc_n=16)
        wo_mm(0)
        for qt in range(1, NO):
            wo_mm(qt)
            wo_ev(qt - 1)
        wo_ev(NO - 1)
    sb_wsl2.free(); sb_att.free(); sb_ot.free()
    for s in sb_xo:
        s.free()

    sb_xqT = AB.alloc(16 * NO * TT, "xqT")
    sb_xqf = AFr.alloc(D, "xq_f32")
    sb_xqb = AB.alloc(D, "xq_b")
    for t in range(NO):
        E("sync", lambda e, t=t: e.dma_start(
            out=sb_xqb.ap(), in_=xq_sp[t * TT:(t + 1) * TT, :]),
            waits=(sV, sLDG), inc=sLD, inc_n=16)
        E("vector", lambda e: e.tensor_copy(out=sb_xqf.ap(), in_=sb_xqb.ap()),
          waits=(sLD, sP), inc=sV)

        def T2(f, t=t):
            pb = (f % 2) * 512
            E("tensor", lambda e, f=f, pb=pb: e.transpose(
                PSA[:, pb:pb + 128], sb_xqf.ap(f * 128, 128), c_iden.ap()),
                waits=(sV,), inc=sP)

        def E2(f, t=t):
            pb = (f % 2) * 512
            E("vector", lambda e, f=f, t=t, pb=pb: e.tensor_copy(
                out=sb_xqT.ap((f * NO + t) * TT, TT), in_=PSA[:, pb:pb + 128]),
                waits=(sP,), inc=sV)
        T2(0)
        for f in range(1, 16):
            T2(f)
            E2(f - 1)
        E2(15)
    sb_xqf.free(); sb_xqb.free()

    sb_dqc = AFr.alloc(16, "deq_cols")
    sb_dqrow = AFr.alloc(128, "deq_row16")
    sb_dqlin = AFr.alloc(2048, "deq_lin")
    sb_dqg = AFr.alloc(1024, "deq_g")
    sb_dqu = AFr.alloc(1024, "deq_u")
    for (mcol, dst) in ((7, 0), (10, 8)):
        E("vector", lambda e, mcol=mcol, dst=dst: e.tensor_scalar(
            out=sb_dqc.ap(dst, 8), in0=sb_mxc.ap(0, 8),
            scalar1=c_sm.ap(mcol, 1), scalar2=-(2.0 ** -7),
            op0=ALU.mult, op1=ALU.mult), inc=sV)
    E("tensor", lambda e: e.transpose(
        PSD[0:16, 0:128], sb_dqc.ap(0, 16), c_iden.ap()), waits=(sV,), inc=sP)
    E("vector", lambda e: e.tensor_copy(
        out=sb_dqrow.ap(0, 128, pn=16), in_=PSD[0:16, 0:128]),
        waits=(sP,), inc=sV)
    for r16 in range(16):
        E("gpsimd", lambda e, r16=r16: e.dma_start(
            out=sb_dqlin.ap(r16 * 128, 128, p0=0, pn=1),
            in_=sb_dqrow.ap(0, 128, p0=r16, pn=1)),
            waits=(sV,), inc=sLDG, inc_n=16)
    for q4 in range(4):
        E("tensor", lambda e, q4=q4: e.matmul(
            PSD[:, 512:1024],
            c_ones.ap(0, 128, p0=0, pn=1),
            sb_dqlin.ap(q4 * 512, 512, p0=0, pn=1),
            start=True, stop=True), waits=(sLDG, sV), inc=sP)
        dq_t = sb_dqg if q4 < 2 else sb_dqu
        off = (q4 % 2) * 512
        E("vector", lambda e, dq_t=dq_t, off=off: e.tensor_copy(
            out=dq_t.ap(off, 512), in_=PSD[:, 512:1024]),
            waits=(sP,), inc=sV)

    sb_dqc.free(); sb_dqrow.free(); sb_dqlin.free()
    sb_mxc.free(); c_ones.free()
    sb_S1 = AB.alloc(64 * 1024, "S1big")
    sb_wg = AFr.alloc(2 * 1024, "wg_f32")
    sb_wgt = AB.alloc(1024, "wg_tern")
    sb_wut = AB.alloc(1024, "wu_tern")
    sb_g1 = AFr.alloc(1024, "g1")
    sb_s1v = AFr.alloc(1024, "silu")
    sb_u1 = AFr.alloc(1024, "u1")
    sb_int = AB.alloc(1024, "inter")
    sb_s1s = AB.alloc(1024, "s1stage")
    sb_tt3 = AFr.alloc(1024, "tern_tmp")

    def tern(src_ap, dst_ap, mh_col, n, waits=()):
        E("vector", lambda e: e.tensor_scalar(
            out=sb_tt3.ap(0, n), in0=src_ap, scalar1=c_sm.ap(mh_col, 1),
            scalar2=1.0, op0=ALU.is_gt, op1=ALU.mult), waits=waits, inc=sV)
        E("vector", lambda e: e.scalar_tensor_tensor(
            out=dst_ap, in0=src_ap, scalar=c_sm.ap(mh_col + 1, 1),
            in1=sb_tt3.ap(0, n), op0=ALU.is_lt, op1=ALU.subtract),
            waits=(sV,), inc=sV)

    GU = [(PSA, PSB), (PSC, PSD)]

    def p5_A(a):
        pg, pu = GU[a % 2]
        for wi, (wT, wtern, mhc, PSX) in enumerate(
                ((wgT, sb_wgt, 5, pg), (wuT, sb_wut, 8, pu))):
            for hh in range(2):
                db = ((a * 4 + wi * 2 + hh) % 2) * 1024
                E("sync", lambda e, wT=wT, a=a, hh=hh, db=db: e.dma_start(
                    out=sb_wg.ap(db, 1024).rearrange("p (c i) -> p c i", i=128),
                    in_=wT[hh * 1024:(hh + 1) * 1024,
                           a * 128:(a + 1) * 128].rearrange(
                        "(c p) i -> p c i", p=128)),
                    waits=(sV,), inc=sLD, inc_n=16)
                W("vector", sLD)
                tern(sb_wg.ap(db, 1024), wtern.ap(), mhc, 1024, waits=(sP,))
                for f in range(8):
                    fg = hh * 8 + f
                    for th in range(2):
                        E("tensor", lambda e, f=f, th=th, fg=fg, wtern=wtern,
                          PSX=PSX: e.matmul(
                            PSX[:, th * 512:(th + 1) * 512],
                            wtern.ap(f * 128, TT),
                            sb_xqT.ap(fg * NO * TT + th * 512, 512),
                            start=(fg == 0), stop=(fg == 15)),
                            waits=((sV,) if f == 0 else ()),
                            inc=(sP if (f == 7 and th == 1) else None))

    def p5_B(a):
        pg, pu = GU[a % 2]
        E("vector", lambda e, pg=pg: e.tensor_tensor(
            out=sb_g1.ap(), in0=pg[:, 0:1024], in1=sb_dqg.ap(),
            op=ALU.mult), waits=(sP, sV), inc=sV)
        E("scalar", lambda e: e.activation(
            out=sb_s1v.ap(), in_=sb_g1.ap(), func=AF.Silu, bias=0.0,
            scale=1.0), waits=(sV,), inc=sS)
        E("vector", lambda e, pu=pu: e.tensor_tensor(
            out=sb_u1.ap(), in0=pu[:, 0:1024], in1=sb_dqu.ap(),
            op=ALU.mult), inc=sV)
        E("vector", lambda e: e.tensor_tensor(
            out=sb_int.ap(), in0=sb_s1v.ap(), in1=sb_u1.ap(), op=ALU.mult),
            waits=(sS,), inc=sV)
        for th in range(2):
            E("tensor", lambda e, th=th, pg=pg: e.matmul(
                pg[:, th * 512:(th + 1) * 512], sb_h128.ap(),
                sb_int.ap(th * 512, 512), start=True, stop=True),
                waits=((sV,) if th == 0 else ()), inc=(sP if th == 1 else None))
        E("vector", lambda e, pg=pg: e.tensor_copy(
            out=sb_s1s.ap(), in_=pg[:, 0:1024]), waits=(sP, sLDG), inc=sV)
        for g in range(2):
            E("gpsimd", lambda e, g=g, a=a: e.dma_start(
                out=sb_S1.ap(0, 64 * 1024, p0=g * 64 + a, pn=1),
                in_=sb_s1s.ap(0, 1024, p0=g * 64, pn=64)),
                waits=(sV,), inc=sLDG, inc_n=16)

    p5_A(0)
    for a in range(1, 64):
        p5_A(a)
        p5_B(a - 1)
    p5_B(63)
    W("vector", m.sLDG)

    m.dbg("S1big", lambda: sb_S1.ap(), [128, 64 * 1024], BF16,
          waits=(sLDG, sV))
    m.dbg("g1", lambda: sb_g1.ap(), [128, 1024], F32, waits=(sV,))
    m.dbg("wgt_last", lambda: sb_wgt.ap(), [128, 1024], BF16, waits=(sV,))
    m.dbg("wut_last", lambda: sb_wut.ap(), [128, 1024], BF16, waits=(sV,))
    m.dbg("dqg", lambda: sb_dqg.ap(), [128, 1024], F32, waits=(sV,))
    m.dbg("inter_last", lambda: sb_int.ap(), [128, 1024], BF16, waits=(sV,))
    if "S1big" in m.debug:
        W("tensor", sST)
        W("vector", sST)

    for s in (sb_wg, sb_wgt, sb_wut, sb_g1, sb_s1v, sb_u1, sb_int, sb_s1s,
              sb_tt3, sb_xqT, sb_dqg, sb_dqu):
        s.free()

    sb_x2f = AFr.alloc(I8K, "x2f")
    sb_u2 = AFr.alloc(512, "u2")
    sb_a2 = AFr.alloc(512, "a2")
    sb_b2 = AFr.alloc(512, "b2")
    sb_q2 = AFr.alloc(16, "q2", high=True)
    sb_dq2 = AFr.alloc(8, "deq2", high=True)

    for t in range(NO):
        for half in range(2):
            for Ti in range(64):
                T = half * 64 + Ti
                g, jj = T >> 6, T & 63
                cg = Ti * 64
                PSX = PS4[cg // 1024]
                col = cg % 1024
                E("tensor", lambda e, g=g, jj=jj, t=t, PSX=PSX, col=col:
                  e.matmul(
                    PSX[:, col:col + 64],
                    sb_S1.ap(jj * 1024 + t * TT, TT, p0=g * 64, pn=64),
                    sb_h64.ap(0, 64, p0=g * 64, pn=64),
                    start=(col % 512 == 0), stop=(col % 512 == 448)),
                    waits=((sV, sLDG) if Ti == 0 else ()),
                    inc=(sP if Ti == 63 else None))
            for pi in range(4):
                E("vector", lambda e, pi=pi: e.reduce_max(
                    out=sb_q2.ap(8 + pi, 1), in_=PS4[pi][:, 0:1024], axis=AX,
                    apply_absolute_value=True), waits=(sP,), inc=sV)
            E("vector", lambda e, half=half: e.tensor_tensor(
                out=sb_q2.ap(half, 1), in0=sb_q2.ap(8, 1), in1=sb_q2.ap(9, 1),
                op=ALU.max), waits=(sV,), inc=sV)
            for pi in (10, 11):
                E("vector", lambda e, pi=pi, half=half: e.tensor_tensor(
                    out=sb_q2.ap(half, 1), in0=sb_q2.ap(half, 1),
                    in1=sb_q2.ap(pi, 1), op=ALU.max), waits=(sV,), inc=sV)
            for pi in range(4):
                E("vector", lambda e, pi=pi, half=half: e.tensor_copy(
                    out=sb_x2f.ap(half * 4096 + pi * 1024, 1024),
                    in_=PS4[pi][:, 0:1024]), inc=sV)
        E("vector", lambda e: e.tensor_tensor(
            out=sb_q2.ap(2, 1), in0=sb_q2.ap(0, 1), in1=sb_q2.ap(1, 1),
            op=ALU.max), waits=(sV,), inc=sV)
        E("vector", lambda e: e.tensor_scalar(
            out=sb_q2.ap(3, 1), in0=sb_q2.ap(2, 1), scalar1=c_sm.ap(3, 1),
            scalar2=1.0, op0=ALU.mult, op1=ALU.mult), waits=(sV,), inc=sV)
        E("vector", lambda e: e.tensor_tensor(
            out=sb_q2.ap(4, 1), in0=sb_q2.ap(3, 1), in1=c_sm.ap(2, 1),
            op=ALU.max), waits=(sV,), inc=sV)
        E("vector", lambda e: e.reciprocal(
            out=sb_q2.ap(5, 1), in_=sb_q2.ap(4, 1)), waits=(sV,), inc=sV)
        E("vector", lambda e: e.tensor_scalar(
            out=sb_q2.ap(5, 1), in0=sb_q2.ap(5, 1), scalar1=128.0,
            scalar2=c_sm.ap(3, 1), op0=ALU.mult, op1=ALU.mult),
            waits=(sV,), inc=sV)
        E("vector", lambda e, t=t: e.tensor_scalar(
            out=sb_dq2.ap(t, 1), in0=sb_q2.ap(4, 1), scalar1=c_sm.ap(13, 1),
            scalar2=-(2.0 ** -7), op0=ALU.mult, op1=ALU.mult),
            waits=(sV,), inc=sV)
        for c0 in range(0, I8K, 512):
            E("vector", lambda e, c0=c0: e.tensor_scalar(
                out=sb_u2.ap(), in0=sb_x2f.ap(c0, 512),
                scalar1=sb_q2.ap(5, 1), scalar2=127.4,
                op0=ALU.mult, op1=ALU.min), waits=(sV,), inc=sV)
            E("vector", lambda e: e.tensor_scalar(
                out=sb_a2.ap(), in0=sb_u2.ap(), scalar1=C23, scalar2=-C23,
                op0=ALU.add, op1=ALU.add), waits=(sV,), inc=sV)
            E("vector", lambda e: e.tensor_scalar(
                out=sb_b2.ap(), in0=sb_u2.ap(), scalar1=C52, scalar2=-C52,
                op0=ALU.add, op1=ALU.add), inc=sV)
            E("vector", lambda e, c0=c0: e.scalar_tensor_tensor(
                out=sb_x2f.ap(c0, 512), in0=sb_a2.ap(), scalar=c_sm.ap(0, 1),
                in1=sb_b2.ap(), op0=ALU.mult, op1=ALU.add),
                waits=(sV,), inc=sV)
        def T3(c, t=t):
            pb = (c % 2) * 512
            E("tensor", lambda e, c=c, pb=pb: e.transpose(
                PSD[:, pb:pb + 128], sb_x2f.ap(c * 128, 128), c_iden.ap()),
                waits=(sV,), inc=sP)

        def E3(c, t=t):
            pb = (c % 2) * 512
            E("vector", lambda e, c=c, t=t, pb=pb: e.tensor_copy(
                out=sb_S1.ap(c * 1024 + t * TT, TT), in_=PSD[:, pb:pb + 128]),
                waits=(sP,), inc=sV)
        T3(0)
        for c in range(1, 64):
            T3(c)
            E3(c - 1)
        E3(63)

    m.dbg("dq2", lambda: sb_dq2.ap(), [128, 8], F32, waits=(sV,))
    for s in (sb_x2f, sb_u2, sb_a2, sb_b2, sb_q2):
        s.free()

    sb_wd = AFr.alloc(2 * 512, "wd_f32")
    sb_wdt = [AB.alloc(512, "wd_t0"), AB.alloc(512, "wd_t1")]
    sb_dt3 = AFr.alloc(512, "wd_tmp")
    sb_do = [AFr.alloc(512, "down_o0"), AFr.alloc(512, "down_o1")]
    for oh in range(4):
        for j in range(64):
            p = j % 2
            E("sync", lambda e, j=j, oh=oh, p=p: e.dma_start(
                out=sb_wd.ap(p * 512, 512),
                in_=wdTp[j * 128:(j + 1) * 128, oh * 512:(oh + 1) * 512]),
                waits=(sV,), inc=sLD, inc_n=16)
            W("vector", sLD)
            E("vector", lambda e, p=p: e.tensor_scalar(
                out=sb_dt3.ap(), in0=sb_wd.ap(p * 512, 512),
                scalar1=c_sm.ap(11, 1), scalar2=1.0, op0=ALU.is_gt, op1=ALU.mult),
                waits=((sP,) if j >= 2 else ()), inc=sV)
            E("vector", lambda e, p=p: e.scalar_tensor_tensor(
                out=sb_wdt[p].ap(), in0=sb_wd.ap(p * 512, 512),
                scalar=c_sm.ap(12, 1), in1=sb_dt3.ap(), op0=ALU.is_lt,
                op1=ALU.subtract), waits=(sV,), inc=sV)
            for t in range(NO):
                PSX = PS4[t // 2]
                col = (t % 2) * 512
                E("tensor", lambda e, j=j, t=t, PSX=PSX, col=col, p=p: e.matmul(
                    PSX[:, col:col + 512],
                    sb_S1.ap(j * 1024 + t * TT, TT),
                    sb_wdt[p].ap(), start=(j == 0), stop=(j == 63)),
                    waits=((sV,) if t == 0 else ()),
                    inc=(sP if t == NO - 1 else None))
        for t in range(NO):
            PSX = PS4[t // 2]
            col = (t % 2) * 512
            E("vector", lambda e, t=t, PSX=PSX, col=col: e.tensor_scalar(
                out=sb_do[t % 2].ap(), in0=PSX[:, col:col + 512],
                scalar1=sb_dq2.ap(t, 1), scalar2=1.0, op0=ALU.mult, op1=ALU.mult),
                waits=(sP, sSTG), inc=sV)
            E("gpsimd", lambda e, t=t, oh=oh: e.dma_start(
                out=out[t * TT:(t + 1) * TT, oh * 512:(oh + 1) * 512],
                in_=sb_do[t % 2].ap(), accum_op=ALU.add),
                waits=(sV, sST), inc=sSTG, inc_n=16)

    W("sync", sST, sSTG, sLDG, sLD, sV, sS, sP)
    W("gpsimd", sSTG)

    with nc.Block() as block:
        for eng in ("sync", "tensor", "vector", "scalar", "gpsimd"):
            getattr(block, eng)(
                lambda e, eng=eng: [fn(e) for fn in m.q[eng]])

    return m


def _f64_split(a):
    hi = a.astype(np.float32)
    lo = (a - hi.astype(np.float64)).astype(np.float32)
    return hi, lo


def _popcount_parity(a, b):
    v = a & b
    p = np.zeros_like(v)
    while np.any(v):
        p ^= (v & 1)
        v >>= 1
    return p


def make_inputs(full):
    x = np.asarray(full["L0_attn_x"])
    Wq = np.asarray(full["L0_attn_Wq"])
    Wk = np.asarray(full["L0_attn_Wk"])
    Wv = np.asarray(full["L0_attn_Wv"])
    Wo = np.asarray(full["L0_attn_Wo"])
    wg = np.asarray(full["L0_ffn_w_gate"])
    wu = np.asarray(full["L0_ffn_w_up"])
    wd = np.asarray(full["L0_ffn_w_down"])

    perm = np.zeros(D, dtype=np.int64)
    for H in range(HEADS):
        j = np.arange(HD)
        perm[H * HD + j] = H * HD + np.where(j < 64, 2 * j, 2 * (j - 64) + 1)
    wqT = np.ascontiguousarray(Wq.T[:, perm])
    wkT = np.ascontiguousarray(Wk.T[:, perm])
    wvT = np.ascontiguousarray(Wv.T)
    woT = np.ascontiguousarray(Wo.T)
    wgT = np.ascontiguousarray(wg.T)
    wuT = np.ascontiguousarray(wu.T)
    r = np.arange(I8K)
    jmap = (r % 64) * 128 + 2 * (r // 128) + ((r % 128) // 64)
    wdTp = np.ascontiguousarray(wd.T[jmap, :])

    i128 = np.arange(128)
    h128m = np.where(_popcount_parity(i128[:, None], i128[None, :]) == 0,
                     1.0, -1.0).astype(BF)
    i64 = np.arange(64)
    h64h = np.where(_popcount_parity(i64[:, None], i64[None, :]) == 0,
                    1.0, -1.0).astype(np.float32)
    h64m = np.tile(h64h, (2, 1)).astype(BF)
    idenf = np.eye(128, dtype=np.float32)

    ch = np.zeros((128, 8), np.float64)
    ch[:, 1] = 1e-5
    ch[:, 2] = 1e-5
    ch[:, 3] = 1.0 / np.sqrt(np.float64(I8K))
    chi, clo = _f64_split(ch)

    in_maps = []
    for c in range(8):
        b, h = c // 2, c % 2
        lo_r = h * 1024 - 512
        xs = np.zeros((NT * TT, D), np.float32)
        src_lo = max(lo_r, 0)
        xs[src_lo - lo_r:, :] = x[b, src_lo:h * 1024 + 1024, :]
        pos = np.arange(lo_r, h * 1024 + 1024, dtype=np.float64)
        pos = np.maximum(pos, 0.0)
        invf = 1.0 / (10000.0 ** (np.arange(0, HD, 2, dtype=np.float64) / HD))
        fr = pos[:, None] * invf[None, :]
        cosv, sinv = np.cos(fr), np.sin(fr)
        sc = 1.0 / np.sqrt(np.float64(HD))
        tvals = {"cq": cosv * sc, "sq": sinv * sc, "ck": cosv, "sk": sinv}
        im = {"x": xs, "wqT": wqT, "wkT": wkT, "wvT": wvT, "woT": woT,
              "wgT": wgT, "wuT": wuT, "wdTp": wdTp,
              "h128": h128m, "h64": h64m, "idenf": idenf,
              "chi": chi, "clo": clo}
        for nm in ("cq", "sq", "ck", "sk"):
            hi, lo2 = _f64_split(tvals[nm])
            im[nm + "_hi"] = hi
            im[nm + "_lo"] = lo2
        mk = np.zeros((NO, 5, TT, TT), np.float32)
        for qt in range(NO):
            pq = h * 1024 + qt * TT + np.arange(TT)
            for s in range(5):
                pk = lo_r + (qt + s) * TT + np.arange(TT)
                mk[qt, s] = ((pk[:, None] <= pq[None, :])
                             & (pq[None, :] - pk[:, None] < 512)
                             & (pk[:, None] >= 0)).astype(np.float32)
        im["maskT"] = mk.reshape(NO * 5 * TT, TT).astype(BF)
        in_maps.append(im)
    return in_maps


def kernel(**inputs):
    in_maps = make_inputs(inputs)
    m = build()
    res = run_bass_kernel_spmd(m.nc, in_maps, core_ids=list(range(8)))
    o = np.zeros((4, 2048, D), np.float32)
    for c in range(8):
        b, h = c // 2, c % 2
        o[b, h * 1024:(h + 1) * 1024, :] = res.results[c]["out"]
    return o
